# version 01
# baseline (speedup 1.0000x reference)
import numpy as np
import ml_dtypes
from contextlib import ExitStack

import concourse.bass as bass
import concourse.mybir as mybir
from concourse.bass_utils import run_bass_kernel_spmd

F32 = mybir.dt.float32
BF16 = mybir.dt.bfloat16
AF = mybir.ActivationFunctionType
ALU = mybir.AluOpType
AX = mybir.AxisListType

D = 2048
KC = 16
TCORE = 2048
HALO = 512
TFULL = TCORE + HALO
QHN = 16
KVH = 4
GQ = 4
DH = 128
WINDOW = 512
NEG = -1e30
NEXP = 16
IEXP = 1408
ICH = IEXP // 128
SI = 2816
SICH = SI // 128
TMACRO = 1024
NS = 16
RDT = 64
TCH = 512
NTCH = TCORE // TCH
E2 = 64

ENGS = ("sync", "gpsimd", "tensor", "vector", "scalar")
SEMS = ("sSD", "sGD", "sTE", "sDV", "sAC", "sPL")


class Prog:

    def __init__(self, nc, ctx):
        self.nc = nc
        self.q = {e: [] for e in ENGS}
        self.sems = {n: ctx.enter_context(nc.semaphore(n)) for n in SEMS}
        self.cnt = {n: 0 for n in SEMS}

    def emit(self, eng, fn):
        self.q[eng].append(fn)

    def inc(self, name, amt):
        self.cnt[name] += amt
        return self.cnt[name]

    def dma(self, eng, out, in_, sem):
        tgt = self.inc(sem, 16)
        s = self.sems[sem]
        self.emit(eng, lambda E, out=out, in_=in_, s=s: E.dma_start(out=out, in_=in_).then_inc(s, 16))
        return tgt

    def dwait(self, eng, sem):
        self.wait(eng, sem, self.cnt[sem])

    def wait(self, eng, sem, tgt):
        if tgt <= 0:
            return
        s = self.sems[sem]
        self.emit(eng, lambda E, s=s, t=tgt: E.wait_ge(s, t))

    def op(self, eng, fn, sem=None, amt=1):
        if sem is None:
            self.emit(eng, lambda E, fn=fn: fn(E))
            return None
        tgt = self.inc(sem, amt)
        s = self.sems[sem]
        self.emit(eng, lambda E, fn=fn, s=s, a=amt: fn(E).then_inc(s, a))
        return tgt

    def vops(self, fn):
        t = self.op("vector", fn, sem="sDV")
        self.wait("vector", "sDV", t)
        return t

    def aops(self, fn):
        t = self.op("scalar", fn, sem="sAC")
        self.wait("scalar", "sAC", t)
        return t

    def barrier(self):
        snap = dict(self.cnt)
        for e in ENGS:
            for n, v in snap.items():
                self.wait(e, n, v)

    def run(self, block):
        qs = self.q

        @block.sync
        def _(E):
            for f in qs["sync"]:
                f(E)

        @block.gpsimd
        def _(E):
            for f in qs["gpsimd"]:
                f(E)

        @block.tensor
        def _(E):
            for f in qs["tensor"]:
                f(E)

        @block.vector
        def _(E):
            for f in qs["vector"]:
                f(E)

        @block.scalar
        def _(E):
            for f in qs["scalar"]:
                f(E)


def blockify(wt, fb):
    K, F = wt.shape
    kc = K // 128
    assert K % 128 == 0 and F % fb == 0
    return np.ascontiguousarray(
        wt.reshape(kc, 128, F // fb, fb).transpose(2, 1, 0, 3).reshape(F // fb, 128, kc * fb)
    )


def bcast_row(p, row_ap, n, ps, out_sb, ones_row_f32, wait_sems=()):
    waits = list(wait_sems) + [("sAC", p.cnt["sAC"])]
    nsl = (n + 511) // 512
    last_ac = None
    for s in range(nsl):
        a, b = s * 512, min(n, (s + 1) * 512)
        for sem, tgt in waits:
            if sem in ("sSD", "sGD"):
                p.dwait("tensor", sem)
            else:
                p.wait("tensor", sem, tgt)
        t_mm = p.op("tensor", lambda E, a=a, b=b: E.matmul(
            ps[:, 0:b - a], ones_row_f32[:], row_ap[:, a:b], start=True, stop=True), sem="sTE")
        p.wait("scalar", "sTE", t_mm)
        last_ac = p.op("scalar", lambda E, a=a, b=b: E.copy(
            out=out_sb[:, a:b], in_=ps[:, 0:b - a]), sem="sAC")
        waits = [("sAC", last_ac)]
    return last_ac


def emit_ln_stats(p, get_sl, T, ones_col_bf, psr0, psr1, sq_scr,
                  mu_row, r_row, tmp_row, eps=1e-5):
    nsl = T // 512
    row_read = 0
    sq_mm = {}
    for s in range(nsl):
        p.wait("tensor", "sDV", row_read)
        for kc in range(KC):
            sl, w = get_sl(kc, s)
            if w is not None:
                (p.dwait("tensor", w[0]) if w[0] in ("sSD", "sGD")
                 else p.wait("tensor", w[0], w[1]))
            p.op("tensor", lambda E, sl=sl, st=(kc == 0), sp=(kc == KC - 1):
                 E.matmul(psr0[:, :], ones_col_bf[:], sl, start=st, stop=sp),
                 sem="sTE" if kc == KC - 1 else None)
        for kc in range(KC):
            sl, w = get_sl(kc, s)
            scr = sq_scr[kc % 2]
            if kc >= 2:
                p.wait("scalar", "sTE", sq_mm[kc - 2])
            elif s > 0:
                p.wait("scalar", "sTE", sq_mm[KC - 2 + kc])
            if w is not None:
                (p.dwait("scalar", w[0]) if w[0] in ("sSD", "sGD")
                 else p.wait("scalar", w[0], w[1]))
            a_t = p.op("scalar", lambda E, scr=scr, sl=sl:
                       E.activation(out=scr[:], in_=sl, func=AF.Square), sem="sAC")
            p.wait("tensor", "sAC", a_t)
            sq_mm[kc] = p.op("tensor", lambda E, scr=scr, st=(kc == 0), sp=(kc == KC - 1):
                             E.matmul(psr1[:, :], ones_col_bf[:], scr[:], start=st, stop=sp),
                             sem="sTE")
        mm_done = p.cnt["sTE"]
        p.wait("vector", "sTE", mm_done)
        mu = mu_row[:, s * 512:(s + 1) * 512]
        rr = r_row[:, s * 512:(s + 1) * 512]
        tp = tmp_row[:, 0:512]
        p.op("vector", lambda E, mu=mu: E.tensor_scalar(
            out=mu, in0=psr0[:, :], scalar1=1.0 / D, scalar2=None, op0=ALU.mult), sem="sDV")
        p.op("vector", lambda E, mu=mu, tp=tp: E.tensor_tensor(
            out=tp, in0=mu, in1=mu, op=ALU.mult), sem="sDV")
        p.op("vector", lambda E, rr=rr: E.tensor_scalar(
            out=rr, in0=psr1[:, :], scalar1=1.0 / D, scalar2=None, op0=ALU.mult), sem="sDV")
        row_read = p.cnt["sDV"]
        p.op("vector", lambda E, rr=rr, tp=tp: E.tensor_tensor(
            out=rr, in0=rr, in1=tp, op=ALU.subtract), sem="sDV")
        p.op("vector", lambda E, rr=rr: E.tensor_scalar(
            out=rr, in0=rr, scalar1=float(eps), scalar2=None, op0=ALU.add), sem="sDV")
        dv_t = p.op("vector", lambda E, rr=rr: E.reciprocal(out=rr, in_=rr), sem="sDV")
        p.wait("scalar", "sDV", dv_t)
        p.op("scalar", lambda E, rr=rr: E.activation(out=rr, in_=rr, func=AF.Sqrt), sem="sAC")
    return p.cnt["sAC"]



def build_launch2(debug=False):
    nc = bass.Bass("TRN2", target_bir_lowering=False)
    stream_in = nc.dram_tensor("stream_in", [KC, 128, TCORE], BF16, kind="ExternalInput").ap()
    dt_in = nc.dram_tensor("dt_in", [KC, 128, TCORE], F32, kind="ExternalInput").ap()
    b_in = nc.dram_tensor("b_in", [NS, TCORE], F32, kind="ExternalInput").ap()
    c_in = nc.dram_tensor("c_in", [NS, TCORE], F32, kind="ExternalInput").ap()
    s_in = nc.dram_tensor("s_in", [128, KC * NS], F32, kind="ExternalInput").ap()
    alog_in = nc.dram_tensor("alog_in", [KC, 128, NS], F32, kind="ExternalInput").ap()
    dvec_in = nc.dram_tensor("dvec_in", [128, KC], F32, kind="ExternalInput").ap()
    rw_in = nc.dram_tensor("rw_in", [1, 128, KC * E2], F32, kind="ExternalInput").ap()
    rb_in = nc.dram_tensor("rb_in", [1, E2], F32, kind="ExternalInput").ap()
    eb_in = nc.dram_tensor("eb_in", [E2, D], F32, kind="ExternalInput").ap()
    ident_in = nc.dram_tensor("ident_in", [128, 128], F32, kind="ExternalInput").ap()

    out = nc.dram_tensor("out", [TCORE, D], F32, kind="ExternalOutput").ap()
    s2_dram = nc.dram_tensor("s2_dram", [KC, 128, TCORE], BF16, kind="Internal").ap()
    if debug:
        dbg_mu = nc.dram_tensor("dbg_mu", [1, TCORE], F32, kind="ExternalOutput").ap()
        dbg_r = nc.dram_tensor("dbg_r", [1, TCORE], F32, kind="ExternalOutput").ap()
        dbg_y = nc.dram_tensor("dbg_y", [KC, 128, TCORE], BF16, kind="ExternalOutput").ap()
        dbg_xn3 = nc.dram_tensor("dbg_xn3", [KC, 128, TCORE], BF16, kind="ExternalOutput").ap()
        dbg_s2 = nc.dram_tensor("dbg_s2", [KC, 128, TCORE], BF16, kind="ExternalOutput").ap()
        dbg_scan = nc.dram_tensor("dbg_scan", [8, 128, TCH], F32, kind="ExternalOutput").ap()
        dbg_y0 = nc.dram_tensor("dbg_y0", [16, 128, TCH], BF16, kind="ExternalOutput").ap()
        dbg_st = nc.dram_tensor("dbg_st", [128, KC * NS], F32, kind="ExternalOutput").ap()
        dbg_xtm = nc.dram_tensor("dbg_xtm", [128, D], BF16, kind="ExternalOutput").ap()
        dbg_xnt = nc.dram_tensor("dbg_xnt", [128, D], BF16, kind="ExternalOutput").ap()
        dbg_lg = nc.dram_tensor("dbg_lg", [128, E2], F32, kind="ExternalOutput").ap()
        dbg_m = nc.dram_tensor("dbg_m", [128, E2], BF16, kind="ExternalOutput").ap()
        dbg_stat = nc.dram_tensor("dbg_stat", [128, 8], F32, kind="ExternalOutput").ap()

    with ExitStack() as ctx:
        S = lambda n_, s, d: ctx.enter_context(nc.sbuf_tensor(n_, s, d))
        PS = lambda n_, s, d: ctx.enter_context(nc.psum_tensor(n_, s, d))
        p = Prog(nc, ctx)

        ps0 = PS("ps0", [128, 512], F32)
        ps1 = PS("ps1", [128, 512], F32)
        ps2 = PS("ps2", [128, 512], F32)
        ps_bf = PS("ps_bf", [128, 512], BF16)
        ps_tq = PS("ps_tq", [64, 128], BF16)
        psr0 = PS("psr0", [1, 512], F32)
        psr1 = PS("psr1", [1, 512], F32)

        ones_col = S("ones_col", [128, 1], BF16)
        ones_row_f32 = S("ones_row_f32", [1, 128], F32)
        ident_bf = S("ident_bf", [128, 128], BF16)
        a_sb = S("a_sb", [128, KC * NS], F32)
        d_sb = S("d_sb", [128, KC], F32)
        state = S("state", [128, KC * NS], F32)
        mu3 = S("mu3", [1, TCORE], F32)
        r3 = S("r3", [1, TCORE], F32)
        tmp_row = S("tmp_row", [1, 512], F32)
        sq_scr = [S(f"sq_scr{i}", [128, 512], BF16) for i in range(2)]
        rw_sb = S("rw_sb", [128, KC * E2], BF16)
        ebias_sb = S("ebias_sb", [E2, D], BF16)
        colsumb = S("colsumb", [128, E2], F32)
        biasb = S("biasb", [128, E2], F32)
        rb_row = S("rb_row", [1, E2], F32)
        cs_row = S("cs_row", [1, E2], F32)

        p.op("vector", lambda E: E.memset(ones_col[:], 1.0), sem="sDV")
        p.op("vector", lambda E: E.memset(ones_row_f32[:], 1.0), sem="sDV")
        ones_dv = p.cnt["sDV"]
        p.dma("gpsimd", ident_bf[:], ident_in, "sGD")
        p.dma("sync", state[:], s_in, "sSD")
        p.dma("sync", d_sb[:], dvec_in, "sSD")
        for kc in range(KC):
            p.dma("sync", a_sb[:, kc * NS:(kc + 1) * NS], alog_in[kc], "sSD")
        t_alog = p.cnt["sSD"]
        p.dwait("scalar", "sSD")
        ac_t = p.op("scalar", lambda E: E.activation(out=a_sb[:], in_=a_sb[:], func=AF.Exp), sem="sAC")
        p.wait("vector", "sAC", ac_t)
        p.op("vector", lambda E: E.tensor_scalar(
            out=a_sb[:], in0=a_sb[:], scalar1=-1.0, scalar2=None, op0=ALU.mult), sem="sDV")
        p.dma("gpsimd", rw_sb[:], rw_in[0], "sGD")
        p.dma("gpsimd", ebias_sb[:], eb_in, "sGD")
        p.dma("sync", rb_row[:], rb_in, "sSD")
        rw_ready = p.cnt["sGD"]
        p.dwait("tensor", "sGD")
        p.wait("tensor", "sDV", ones_dv)
        for kc in range(KC):
            p.op("tensor", lambda E, kc=kc, st=(kc == 0), sp=(kc == KC - 1):
                 E.matmul(psr0[:, 0:E2], ones_col[:], rw_sb[:, kc * E2:(kc + 1) * E2],
                          start=st, stop=sp),
                 sem="sTE" if kc == KC - 1 else None)
        p.wait("vector", "sTE", p.cnt["sTE"])
        cs_dv = p.op("vector", lambda E: E.tensor_copy(out=cs_row[:], in_=psr0[:, 0:E2]), sem="sDV")
        bcast_row(p, cs_row, E2, ps0, colsumb, ones_row_f32,
                  wait_sems=[("sDV", cs_dv)])
        bcast_row(p, rb_row, E2, ps0, biasb, ones_row_f32,
                  wait_sems=[("sSD", p.cnt["sSD"])])
        p.barrier()

        with ExitStack() as q1ctx:
            xs1 = q1ctx.enter_context(nc.sbuf_tensor("xs1", [128, KC * TCORE], BF16))
            ln3_loads = {}
            for kc in range(KC):
                tgt = p.dma("sync", xs1[:, kc * TCORE:(kc + 1) * TCORE], stream_in[kc], "sSD")
                ln3_loads[kc] = (xs1[:, kc * TCORE:(kc + 1) * TCORE], ("sSD", tgt))
            emit_ln_stats(p, lambda kc, s: (ln3_loads[kc][0][:, s * 512:(s + 1) * 512],
                                            ln3_loads[kc][1]),
                          TCORE, ones_col, psr0, psr1,
                          sq_scr, mu3, r3, tmp_row, eps=1e-5)
            p.barrier()
            if debug:
                p.dma("sync", dbg_mu, mu3[:], "sSD")
                p.dma("sync", dbg_r, r3[:], "sSD")

        with ExitStack() as q2ctx:
            S2 = lambda n_, s, d: q2ctx.enter_context(nc.sbuf_tensor(n_, s, d))
            xn3 = S2("xn3", [128, KC * TCH], BF16)
            dt_sb = S2("dt_sb", [128, KC * TCH], F32)
            w_sb = S2("w_sb", [128, KC * TCH], BF16)
            y_sb = S2("y_sb", [128, KC * TCH], BF16)
            mu3b = S2("mu3b", [128, TCH], F32)
            r3b = S2("r3b", [128, TCH], F32)
            xsc = [S2(f"xsc{i}", [128, TCH], BF16) for i in range(2)]
            a_t = [S2(f"a_t{i}", [128, TCH], F32) for i in range(2)]
            g_t = [S2(f"g_t{i}", [128, TCH], F32) for i in range(2)]
            x_t = [S2(f"x_t{i}", [128, TCH], F32) for i in range(2)]
            bb = S2("bb", [128, TCH], F32)
            cb = S2("cb", [128, TCH], F32)
            b_row = S2("b_row", [1, TCH], F32)
            c_row = S2("c_row", [1, TCH], F32)

            for tc in range(NTCH):
                t0 = tc * TCH
                bcast_row(p, mu3[:, t0:t0 + TCH], TCH, ps0, mu3b, ones_row_f32)
                bcast_row(p, r3[:, t0:t0 + TCH], TCH, ps0, r3b, ones_row_f32)
                bc_done = p.cnt["sAC"]
                xsc_read = {}
                for kc in range(KC):
                    buf = xsc[kc % 2]
                    if kc >= 2:
                        p.wait("sync", "sDV", xsc_read[kc - 2])
                    ld = p.dma("sync", buf[:], stream_in[kc, :, t0:t0 + TCH], "sSD")
                    ldt = p.dma("sync", dt_sb[:, kc * TCH:(kc + 1) * TCH],
                                dt_in[kc, :, t0:t0 + TCH], "sSD")
                    p.dwait("vector", "sSD")
                    p.wait("vector", "sAC", bc_done)
                    sl = xn3[:, kc * TCH:(kc + 1) * TCH]
                    xsc_read[kc] = p.op("vector", lambda E, buf=buf: E.tensor_tensor(
                        out=a_t[0][:], in0=buf[:], in1=mu3b[:], op=ALU.subtract), sem="sDV")
                    p.op("vector", lambda E, sl=sl: E.tensor_tensor(
                        out=sl, in0=a_t[0][:], in1=r3b[:], op=ALU.mult), sem="sDV")
                    p.dwait("vector", "sSD")
                    p.op("vector", lambda E, sl=sl, kc=kc: E.tensor_tensor(
                        out=w_sb[:, kc * TCH:(kc + 1) * TCH],
                        in0=dt_sb[:, kc * TCH:(kc + 1) * TCH], in1=sl, op=ALU.mult), sem="sDV")
                prep_done = p.cnt["sDV"]
                scan_done = {}
                pool_done = {}
                for n in range(NS):
                    p.wait("sync", "sTE", p.cnt["sTE"])
                    ldb = p.dma("sync", b_row[:], b_in[n:n + 1, t0:t0 + TCH], "sSD")
                    ldc = p.dma("sync", c_row[:], c_in[n:n + 1, t0:t0 + TCH], "sSD")
                    prev_dv = p.cnt["sDV"]
                    bcast_row(p, b_row, TCH, ps1, bb, ones_row_f32,
                              wait_sems=[("sSD", ldb), ("sDV", prev_dv)])
                    bcast_row(p, c_row, TCH, ps1, cb, ones_row_f32,
                              wait_sems=[("sSD", ldc)])
                    bcn_done = p.cnt["sAC"]
                    for g in range(KC):
                        ii = n * KC + g
                        ab = a_t[ii % 2]
                        gb = g_t[ii % 2]
                        xb = x_t[ii % 2]
                        col = g * NS + n
                        dsl = dt_sb[:, g * TCH:(g + 1) * TCH]
                        if ii >= 2:
                            p.wait("scalar", "sDV", scan_done[ii - 2])
                        else:
                            p.wait("scalar", "sDV", prep_done)
                        at_t = p.op("scalar", lambda E, ab=ab, dsl=dsl, col=col:
                                    E.activation(out=ab[:], in_=dsl, func=AF.Exp,
                                                 scale=a_sb[:, col:col + 1]), sem="sAC")
                        p.wait("vector", "sAC", bcn_done)
                        p.op("vector", lambda E, gb=gb, g=g: E.tensor_tensor(
                            out=gb[:], in0=w_sb[:, g * TCH:(g + 1) * TCH], in1=bb[:],
                            op=ALU.mult), sem="sDV")
                        p.wait("vector", "sAC", at_t)
                        if ii >= 2:
                            p.wait("vector", "sPL", pool_done[ii - 2])
                        sc_t = p.op("vector", lambda E, xb=xb, ab=ab, gb=gb, col=col:
                                    E.tensor_tensor_scan(
                                        out=xb[:], data0=ab[:], data1=gb[:],
                                        initial=state[:, col:col + 1],
                                        op0=ALU.mult, op1=ALU.add), sem="sDV")
                        if n == 0:
                            p.op("vector", lambda E, g=g: E.tensor_scalar(
                                out=y_sb[:, g * TCH:(g + 1) * TCH],
                                in0=xn3[:, g * TCH:(g + 1) * TCH],
                                scalar1=d_sb[:, g:g + 1], scalar2=None, op0=ALU.mult), sem="sDV")
                        p.op("vector", lambda E, xb=xb, gb=gb: E.tensor_tensor(
                            out=gb[:], in0=xb[:], in1=cb[:], op=ALU.mult), sem="sDV")
                        ydone = p.op("vector", lambda E, gb=gb, g=g: E.tensor_tensor(
                            out=y_sb[:, g * TCH:(g + 1) * TCH],
                            in0=y_sb[:, g * TCH:(g + 1) * TCH], in1=gb[:], op=ALU.add), sem="sDV")
                        scan_done[ii] = ydone
                        if debug and tc == 1 and g == 0:
                            p.wait("sync", "sDV", ydone)
                            p.dma("sync", dbg_y0[n], y_sb[:, 0:TCH], "sSD")
                        if debug and tc == 0 and n == 0 and g < 2:
                            p.wait("sync", "sDV", sc_t)
                            p.dma("sync", dbg_scan[3 * g + 0], ab[:], "sSD")
                            p.dma("sync", dbg_scan[3 * g + 1], xb[:], "sSD")
                            if g == 0:
                                p.dma("sync", dbg_scan[6], bb[:], "sSD")
                                p.dma("sync", dbg_scan[7], cb[:], "sSD")
                        p.wait("gpsimd", "sDV", sc_t)
                        pool_done[ii] = p.op("gpsimd", lambda E, xb=xb, col=col: E.tensor_copy(
                            out=state[:, col:col + 1], in_=xb[:, TCH - 1:TCH]), sem="sPL")
                if debug:
                    p.wait("sync", "sDV", p.cnt["sDV"])
                    for kc in range(KC):
                        p.dma("sync", dbg_y[kc, :, t0:t0 + TCH],
                              y_sb[:, kc * TCH:(kc + 1) * TCH], "sSD")
                        p.dma("sync", dbg_xn3[kc, :, t0:t0 + TCH],
                              xn3[:, kc * TCH:(kc + 1) * TCH], "sSD")
                if debug and tc == 0:
                    pass
                upd_store = {}
                for kc in range(KC):
                    buf = xsc[kc % 2]
                    if kc >= 2:
                        p.dwait("sync", "sSD")
                    ld = p.dma("sync", buf[:], stream_in[kc, :, t0:t0 + TCH], "sSD")
                    p.dwait("vector", "sSD")
                    upd = p.op("vector", lambda E, buf=buf, kc=kc: E.tensor_tensor(
                        out=buf[:], in0=buf[:], in1=y_sb[:, kc * TCH:(kc + 1) * TCH],
                        op=ALU.add), sem="sDV")
                    p.wait("sync", "sDV", upd)
                    upd_store[kc] = p.dma("sync", s2_dram[kc, :, t0:t0 + TCH], buf[:], "sSD")
                p.barrier()
                if debug and tc == 0:
                    p.dma("sync", dbg_st, state[:], "sSD")
                    p.barrier()

        with ExitStack() as q3ctx:
            S3 = lambda n_, s, d: q3ctx.enter_context(nc.sbuf_tensor(n_, s, d))
            xs = S3("xs", [128, KC * TCORE], BF16)
            for kc in range(KC):
                p.dma("sync", xs[:, kc * TCORE:(kc + 1) * TCORE], s2_dram[kc], "sSD")
            p.barrier()
            if debug:
                for kc in range(KC):
                    p.dma("sync", dbg_s2[kc], xs[:, kc * TCORE:(kc + 1) * TCORE], "sSD")

            x_tm = [S3(f"x_tm{i}", [128, D], BF16) for i in range(2)]
            xn_tm = [S3(f"xn_tm{i}", [128, D], BF16) for i in range(2)]
            sqf = [S3(f"sqf{i}", [128, D], BF16) for i in range(2)]
            mt_sb = [S3(f"mt_sb{i}", [64, 128], BF16) for i in range(2)]
            lgf = [S3(f"lgf{i}", [128, E2], F32) for i in range(2)]
            m8 = [S3(f"m8{i}", [128, 8], F32) for i in range(2)]
            stat = [S3(f"stat{i}", [128, 8], F32) for i in range(2)]
            otile = [S3(f"otile{i}", [128, 512], F32) for i in range(4)]
            otile_free = [0, 0, 0, 0]
            prev_dv = {}
            ps0_free = 0
            psx_free = {0: 0, 1: 0}

            for tt in range(TCORE // 128):
                pp = tt % 2
                xtm = x_tm[pp]
                xnt = xn_tm[pp]
                sq = sqf[pp]
                st_ = stat[pp]
                lg = lgf[pp]
                if tt >= 2:
                    p.wait("tensor", "sDV", prev_dv[tt - 2])
                for kq in range(4):
                    for j in range(4):
                        kc = kq * 4 + j
                        sl = xs[:, kc * TCORE + tt * 128: kc * TCORE + tt * 128 + 128]
                        p.op("tensor", lambda E, sl=sl, j=j: E.transpose(
                            ps_bf[:, j * 128:(j + 1) * 128], sl, ident_bf[:]),
                            sem="sTE" if j == 3 else None)
                    tp_done = p.cnt["sTE"]
                    p.wait("vector", "sTE", tp_done)
                    ev = p.op("vector", lambda E, kq=kq, xtm=xtm: E.tensor_copy(
                        out=xtm[:, kq * 512:(kq + 1) * 512], in_=ps_bf[:]), sem="sDV")
                    p.wait("tensor", "sDV", ev)
                p.vops(lambda E, xtm=xtm, st_=st_: E.tensor_reduce(
                    out=st_[:, 0:1], in_=xtm[:], axis=AX.X, op=ALU.add))
                p.vops(lambda E, st_=st_: E.tensor_scalar(
                    out=st_[:, 0:1], in0=st_[:, 0:1], scalar1=1.0 / D, scalar2=None,
                    op0=ALU.mult))
                dv_t = p.cnt["sDV"]
                p.wait("scalar", "sDV", dv_t)
                ac_t = p.op("scalar", lambda E, xtm=xtm, sq=sq, st_=st_: E.activation(
                    out=sq[:], in_=xtm[:], func=AF.Square, accum_out=st_[:, 3:4]), sem="sAC")
                p.wait("vector", "sAC", ac_t)
                p.vops(lambda E, st_=st_: E.tensor_scalar(
                    out=st_[:, 3:4], in0=st_[:, 3:4], scalar1=1.0 / D, scalar2=None,
                    op0=ALU.mult))
                p.vops(lambda E, st_=st_: E.scalar_tensor_tensor(
                    out=st_[:, 1:2], in0=st_[:, 0:1], scalar=st_[:, 0:1], in1=st_[:, 3:4],
                    op0=ALU.mult, op1=ALU.subtract))
                p.vops(lambda E, st_=st_: E.tensor_scalar(
                    out=st_[:, 1:2], in0=st_[:, 1:2], scalar1=-1.0, scalar2=1e-5,
                    op0=ALU.mult, op1=ALU.add))
                rc_t = p.vops(lambda E, st_=st_: E.reciprocal(
                    out=st_[:, 1:2], in_=st_[:, 1:2]))
                p.wait("scalar", "sDV", rc_t)
                sqr_t = p.op("scalar", lambda E, st_=st_: E.activation(
                    out=st_[:, 1:2], in_=st_[:, 1:2], func=AF.Sqrt), sem="sAC")
                p.wait("vector", "sAC", sqr_t)
                p.vops(lambda E, st_=st_: E.tensor_scalar(
                    out=st_[:, 2:3], in0=st_[:, 1:2], scalar1=-1.0, scalar2=None,
                    op0=ALU.mult))
                p.op("vector", lambda E, xtm=xtm, xnt=xnt, st_=st_: E.tensor_scalar(
                    out=xnt[:], in0=xtm[:], scalar1=st_[:, 0:1], scalar2=st_[:, 1:2],
                    op0=ALU.subtract, op1=ALU.mult), sem="sDV")
                p.wait("tensor", "sDV", ps0_free)
                for kc in range(KC):
                    sl = xs[:, kc * TCORE + tt * 128: kc * TCORE + tt * 128 + 128]
                    p.op("tensor", lambda E, sl=sl, kc=kc, st=(kc == 0), sp=(kc == KC - 1):
                         E.matmul(ps0[:, 0:E2], sl, rw_sb[:, kc * E2:(kc + 1) * E2],
                                  start=st, stop=sp),
                         sem="sTE" if kc == KC - 1 else None)
                rt_done = p.cnt["sTE"]
                p.wait("vector", "sTE", rt_done)
                p.op("vector", lambda E, lg=lg, st_=st_: E.scalar_tensor_tensor(
                    out=lg[:], in0=colsumb[:], scalar=st_[:, 0:1], in1=ps0[:, 0:E2],
                    op0=ALU.mult, op1=ALU.subtract), sem="sDV")
                ps0_free = p.cnt["sDV"]
                p.op("vector", lambda E, lg=lg, st_=st_: E.tensor_scalar(
                    out=lg[:], in0=lg[:], scalar1=st_[:, 2:3], scalar2=None,
                    op0=ALU.mult), sem="sDV")
                p.op("vector", lambda E, lg=lg: E.tensor_tensor(
                    out=lg[:], in0=lg[:], in1=biasb[:], op=ALU.add), sem="sDV")
                p.vops(lambda E, lg=lg, st_=st_: E.reduce_max(
                    out=st_[:, 4:5], in_=lg[:], axis=AX.X, negate=True))
                dv_t = p.cnt["sDV"]
                p.wait("scalar", "sDV", dv_t)
                ex_t = p.op("scalar", lambda E, lg=lg, st_=st_: E.activation(
                    out=lg[:], in_=lg[:], func=AF.Exp, bias=st_[:, 4:5],
                    accum_out=st_[:, 5:6]), sem="sAC")
                p.wait("vector", "sAC", ex_t)
                p.vops(lambda E, st_=st_: E.reciprocal(
                    out=st_[:, 6:7], in_=st_[:, 5:6]))
                p.op("vector", lambda E, lg=lg, st_=st_: E.tensor_scalar(
                    out=lg[:], in0=lg[:], scalar1=st_[:, 6:7], scalar2=None,
                    op0=ALU.mult), sem="sDV")
                p.vops(lambda E, lg=lg, m8_=m8[pp]: E.max(out=m8_[:], in_=lg[:]))
                mbf_t = p.op("vector", lambda E, lg=lg, m8_=m8[pp], sq=sq: E.scalar_tensor_tensor(
                    out=sq[:, 0:E2], in0=lg[:], scalar=m8_[:, 7:8], in1=lg[:],
                    op0=ALU.is_ge, op1=ALU.mult), sem="sDV")
                p.vops(lambda E, sq=sq, st_=st_: E.tensor_reduce(
                    out=st_[:, 7:8], in_=sq[:, 0:E2], axis=AX.X, op=ALU.add))
                if debug and tt == 0:
                    p.wait("sync", "sDV", p.cnt["sDV"])
                    p.dma("sync", dbg_xtm, xtm[:], "sSD")
                    p.dma("sync", dbg_xnt, xnt[:], "sSD")
                    p.dma("sync", dbg_lg, lg[:], "sSD")
                    p.dma("sync", dbg_m, sq[:, 0:E2], "sSD")
                    p.dma("sync", dbg_stat, st_[:], "sSD")
                p.wait("tensor", "sDV", mbf_t)
                mt_t = p.op("tensor", lambda E, sq=sq: E.transpose(
                    ps_tq[:], sq[:, 0:E2], ident_bf[:]), sem="sTE")
                p.wait("vector", "sTE", mt_t)
                mte_t = p.op("vector", lambda E, mt=mt_sb[pp]: E.tensor_copy(
                    out=mt[:], in_=ps_tq[:]), sem="sDV")
                p.wait("tensor", "sDV", mte_t)
                for hf in range(4):
                    pidx = hf % 2
                    psx = ps1 if pidx == 0 else ps2
                    p.wait("tensor", "sDV", psx_free[pidx])
                    mm_t = p.op("tensor", lambda E, mt=mt_sb[pp], hf=hf, psx=psx: E.matmul(
                        psx[:, :], mt[:], ebias_sb[:, hf * 512:(hf + 1) * 512],
                        start=True, stop=True), sem="sTE")
                    oi = (tt * 4 + hf) % 4
                    ot = otile[oi]
                    p.dwait("vector", "sSD")
                    p.wait("vector", "sTE", mm_t)
                    psx_free[pidx] = p.op("vector", lambda E, ot=ot, xnt=xnt, st_=st_, hf=hf, psx=psx:
                                          E.scalar_tensor_tensor(
                                              out=ot[:], in0=xnt[:, hf * 512:(hf + 1) * 512],
                                              scalar=st_[:, 7:8], in1=psx[:, :],
                                              op0=ALU.mult, op1=ALU.add), sem="sDV")
                    fin_t = p.op("vector", lambda E, ot=ot, xtm=xtm, hf=hf: E.tensor_tensor(
                        out=ot[:], in0=ot[:], in1=xtm[:, hf * 512:(hf + 1) * 512],
                        op=ALU.add), sem="sDV")
                    p.wait("sync", "sDV", fin_t)
                    otile_free[oi] = p.dma(
                        "sync", out[tt * 128:(tt + 1) * 128, hf * 512:(hf + 1) * 512], ot[:], "sSD")
                prev_dv[tt] = p.cnt["sDV"]

            p.barrier()

        block = ctx.enter_context(nc.Block())
        p.run(block)
    return nc



MOE_GROUPS = [[0, 1, 2], [3, 4, 5], [6, 7, 8], [9, 10, 11], [12, 13, 14], [15]]


def build_launch1(debug=False):
    nc = bass.Bass("TRN2", target_bir_lowering=False)
    xfm = nc.dram_tensor("xfm", [KC, 128, TFULL], F32, kind="ExternalInput").ap()
    wq_in = nc.dram_tensor("wq_in", [QHN, 128, D], F32, kind="ExternalInput").ap()
    wk_in = nc.dram_tensor("wk_in", [1, 128, KC * 512], F32, kind="ExternalInput").ap()
    wv_in = nc.dram_tensor("wv_in", [1, 128, KC * 512], F32, kind="ExternalInput").ap()
    wo_in = nc.dram_tensor("wo_in", [KC, 128, D], F32, kind="ExternalInput").ap()
    qn_in = nc.dram_tensor("qn_in", [128, 1], F32, kind="ExternalInput").ap()
    kn_in = nc.dram_tensor("kn_in", [128, 1], F32, kind="ExternalInput").ap()
    cos_in = nc.dram_tensor("cos_in", [128, TFULL], BF16, kind="ExternalInput").ap()
    sin_in = nc.dram_tensor("sin_in", [128, TFULL], BF16, kind="ExternalInput").ap()
    mask_in = nc.dram_tensor("mask_in", [16, 128, 5 * 512], BF16, kind="ExternalInput").ap()
    gate_in = nc.dram_tensor("gate_in", [1, 128, KC * NEXP], F32, kind="ExternalInput").ap()
    eup_in = nc.dram_tensor("eup_in", [NEXP, 22, 128, D], F32, kind="ExternalInput").ap()
    edn_in = nc.dram_tensor("edn_in", [NEXP, KC, 128, IEXP], F32, kind="ExternalInput").ap()
    sup_in = nc.dram_tensor("sup_in", [44, 128, D], F32, kind="ExternalInput").ap()
    sdn_in = nc.dram_tensor("sdn_in", [KC, 128, SI], F32, kind="ExternalInput").ap()
    wdtin_in = nc.dram_tensor("wdtin_in", [1, 128, KC * RDT], F32, kind="ExternalInput").ap()
    wdtout_in = nc.dram_tensor("wdtout_in", [RDT, D], F32, kind="ExternalInput").ap()
    bdt_in = nc.dram_tensor("bdt_in", [128, KC], F32, kind="ExternalInput").ap()
    wb_in = nc.dram_tensor("wb_in", [1, 128, KC * NS], F32, kind="ExternalInput").ap()
    wc_in = nc.dram_tensor("wc_in", [1, 128, KC * NS], F32, kind="ExternalInput").ap()
    alog_in = nc.dram_tensor("alog_in", [KC, 128, NS], F32, kind="ExternalInput").ap()
    ident_in = nc.dram_tensor("ident_in", [128, 128], F32, kind="ExternalInput").ap()
    rotm_in = nc.dram_tensor("rotm_in", [128, 128], F32, kind="ExternalInput").ap()

    stream_out = nc.dram_tensor("stream_out", [KC, 128, TCORE], BF16, kind="ExternalOutput").ap()
    dt_out = nc.dram_tensor("dt_out", [KC, 128, TCORE], F32, kind="ExternalOutput").ap()
    b_out = nc.dram_tensor("b_out", [NS, TCORE], F32, kind="ExternalOutput").ap()
    c_out = nc.dram_tensor("c_out", [NS, TCORE], F32, kind="ExternalOutput").ap()
    s_out = nc.dram_tensor("s_out", [128, KC * NS], F32, kind="ExternalOutput").ap()

    s_attn = nc.dram_tensor("s_attn", [KC, 128, TCORE], BF16, kind="Internal").ap()
    cw_dram = nc.dram_tensor("cw_dram", [NEXP, TCORE], F32, kind="Internal").ap()
    if debug:
        dbg_sattn = nc.dram_tensor("dbg_sattn", [KC, 128, TCORE], BF16, kind="ExternalOutput").ap()
        dbg_cw = nc.dram_tensor("dbg_cw", [NEXP, TCORE], F32, kind="ExternalOutput").ap()
        dbg_xn1 = nc.dram_tensor("dbg_xn1", [KC, 128, TFULL], BF16, kind="ExternalOutput").ap()
        dbg_k = nc.dram_tensor("dbg_k", [128, KVH * TFULL], BF16, kind="ExternalOutput").ap()
        dbg_v = nc.dram_tensor("dbg_v", [128, 20 * 512], BF16, kind="ExternalOutput").ap()
        dbg_xn2 = nc.dram_tensor("dbg_xn2", [KC, 128, TCORE], BF16, kind="ExternalOutput").ap()
        dbg_sm0 = nc.dram_tensor("dbg_sm0", [KC, 128, TMACRO], BF16, kind="ExternalOutput").ap()

    with ExitStack() as ctx:
        S = lambda n_, s, d: ctx.enter_context(nc.sbuf_tensor(n_, s, d))
        PS = lambda n_, s, d: ctx.enter_context(nc.psum_tensor(n_, s, d))
        p = Prog(nc, ctx)

        ps = [PS(f"ps{i}", [128, 512], F32) for i in range(6)]
        psr0 = PS("psr0", [1, 512], F32)
        psr1 = PS("psr1", [1, 512], F32)

        ones_col = S("ones_col", [128, 1], BF16)
        ones_row_f32 = S("ones_row_f32", [1, 128], F32)
        ident_f32 = S("ident_f32", [128, 128], F32)
        rotm_sb = S("rotm_sb", [128, 128], BF16)
        qw_col = S("qw_col", [128, 1], F32)
        kw_col = S("kw_col", [128, 1], F32)
        tmp_row = S("tmp_row", [1, 512], F32)
        sq_scr = [S(f"sq_scr{i}", [128, 512], BF16) for i in range(2)]

        p.op("vector", lambda E: E.memset(ones_col[:], 1.0), sem="sDV")
        p.op("vector", lambda E: E.memset(ones_row_f32[:], 1.0), sem="sDV")
        p.dma("sync", ident_f32[:], ident_in, "sSD")
        p.dma("gpsimd", rotm_sb[:], rotm_in, "sGD")
        p.dma("sync", qw_col[:], qn_in, "sSD")
        p.dma("sync", kw_col[:], kn_in, "sSD")
        p.dwait("vector", "sSD")
        p.vops(lambda E: E.tensor_scalar(
            out=qw_col[:], in0=qw_col[:], scalar1=1.0, scalar2=float(DH ** -0.5),
            op0=ALU.add, op1=ALU.mult))
        p.vops(lambda E: E.tensor_scalar(
            out=kw_col[:], in0=kw_col[:], scalar1=1.0, scalar2=None, op0=ALU.add))
        p.barrier()

        with ExitStack() as actx:
            SA = lambda n_, s, d: actx.enter_context(nc.sbuf_tensor(n_, s, d))
            xn1 = SA("xn1", [128, KC * TCORE], BF16)
            K_sb = SA("K_sb", [128, KVH * TFULL], BF16)
            V_sb = SA("V_sb", [128, 20 * 512], BF16)
            cos_sb = SA("cos_sb", [128, TFULL], BF16)
            sin_sb = SA("sin_sb", [128, TFULL], BF16)
            rstdb = SA("rstdb", [128, 512], F32)
            recipb = SA("recipb", [128, 512], F32)
            tmpq = SA("tmpq", [128, 512], BF16)
            ropeA = SA("ropeA", [128, 512], BF16)
            ropeB = SA("ropeB", [128, 512], BF16)
            den_r = SA("den_r", [1, 512], F32)
            rope_ps = None
            hctx = ExitStack()
            xn1h = hctx.enter_context(nc.sbuf_tensor("xn1h", [128, KC * HALO], BF16))

            def xsl(kc, s):
                if s == 0:
                    return xn1h[:, kc * HALO:(kc + 1) * HALO]
                return xn1[:, kc * TCORE + (s - 1) * 512: kc * TCORE + s * 512]

            for kc in range(KC):
                p.dma("gpsimd", xn1h[:, kc * HALO:(kc + 1) * HALO],
                      xfm[kc, :, 0:HALO], "sGD")
                p.dma("gpsimd", xn1[:, kc * TCORE:(kc + 1) * TCORE],
                      xfm[kc, :, HALO:TFULL], "sGD")
            with ExitStack() as bctx:
                mu1 = bctx.enter_context(nc.sbuf_tensor("mu1", [1, TFULL], F32))
                r1 = bctx.enter_context(nc.sbuf_tensor("r1", [1, TFULL], F32))
                emit_ln_stats(p, lambda kc, s: (xsl(kc, s), ("sGD", None)),
                              TFULL, ones_col, psr0, psr1,
                              sq_scr, mu1, r1, tmp_row, eps=1e-5)
                mu1b = bctx.enter_context(nc.sbuf_tensor("mu1b", [128, TFULL], F32))
                r1b = bctx.enter_context(nc.sbuf_tensor("r1b", [128, TFULL], F32))
                t1s = bctx.enter_context(nc.sbuf_tensor("t1s", [128, TFULL], BF16))
                bcast_row(p, mu1, TFULL, ps[5], mu1b, ones_row_f32)
                bc = bcast_row(p, r1, TFULL, ps[5], r1b, ones_row_f32)
                p.wait("vector", "sAC", bc)
                for kc in range(KC):
                    slh = xn1h[:, kc * HALO:(kc + 1) * HALO]
                    p.op("vector", lambda E, slh=slh: E.tensor_tensor(
                        out=t1s[:, 0:HALO], in0=slh, in1=mu1b[:, 0:HALO],
                        op=ALU.subtract), sem="sDV")
                    p.op("vector", lambda E, slh=slh: E.tensor_tensor(
                        out=slh, in0=t1s[:, 0:HALO], in1=r1b[:, 0:HALO],
                        op=ALU.mult), sem="sDV")
                    sl = xn1[:, kc * TCORE:(kc + 1) * TCORE]
                    p.op("vector", lambda E, sl=sl: E.tensor_tensor(
                        out=t1s[:, 0:TCORE], in0=sl, in1=mu1b[:, HALO:TFULL],
                        op=ALU.subtract), sem="sDV")
                    p.op("vector", lambda E, sl=sl: E.tensor_tensor(
                        out=sl, in0=t1s[:, 0:TCORE], in1=r1b[:, HALO:TFULL],
                        op=ALU.mult), sem="sDV")
                p.barrier()
            if debug:
                for kc in range(KC):
                    p.dma("sync", dbg_xn1[kc, :, 0:HALO],
                          xn1h[:, kc * HALO:(kc + 1) * HALO], "sSD")
                    p.dma("sync", dbg_xn1[kc, :, HALO:TFULL],
                          xn1[:, kc * TCORE:(kc + 1) * TCORE], "sSD")

            rope_ps = ps[2]
            p.dma("sync", cos_sb[:], cos_in, "sSD")
            p.dma("sync", sin_sb[:], sin_in, "sSD")

            def qk_evac(psq, wcol, out_sl, cos_a, cos_b):
                aq = p.op("scalar", lambda E, psq=psq: E.activation(
                    out=sq_scr[0][:], in_=psq[:], func=AF.Square), sem="sAC")
                p.wait("tensor", "sAC", aq)
                mmq = p.op("tensor", lambda E: E.matmul(
                    psr0[:, :], ones_col[:], sq_scr[0][:], start=True, stop=True), sem="sTE")
                p.wait("vector", "sTE", mmq)
                p.vops(lambda E: E.tensor_scalar(
                    out=tmp_row[:, :], in0=psr0[:, :], scalar1=1.0 / DH, scalar2=1e-6,
                    op0=ALU.mult, op1=ALU.add))
                rc = p.vops(lambda E: E.reciprocal(out=tmp_row[:, :], in_=tmp_row[:, :]))
                p.wait("scalar", "sDV", rc)
                sq_t = p.aops(lambda E: E.activation(
                    out=tmp_row[:, :], in_=tmp_row[:, :], func=AF.Sqrt))
                p.wait("tensor", "sAC", sq_t)
                bmm = p.op("tensor", lambda E: E.matmul(
                    ps[5][:, :], ones_row_f32[:], tmp_row[:, :], start=True, stop=True), sem="sTE")
                p.wait("scalar", "sTE", bmm)
                bc2 = p.op("scalar", lambda E: E.copy(out=rstdb[:], in_=ps[5][:, :]), sem="sAC")
                p.wait("vector", "sAC", bc2)
                tq_t = p.op("vector", lambda E, psq=psq, wcol=wcol: E.scalar_tensor_tensor(
                    out=tmpq[:], in0=psq[:], scalar=wcol[:], in1=rstdb[:],
                    op0=ALU.mult, op1=ALU.mult), sem="sDV")
                p.wait("tensor", "sDV", p.cnt["sDV"])
                rmm = p.op("tensor", lambda E: E.matmul(
                    rope_ps[:, :], rotm_sb[:], tmpq[:], start=True, stop=True), sem="sTE")
                p.wait("vector", "sTE", rmm)
                p.op("vector", lambda E, sb_=cos_b: E.tensor_tensor(
                    out=ropeA[:], in0=rope_ps[:, :], in1=sb_, op=ALU.mult), sem="sDV")
                p.op("vector", lambda E, ca=cos_a: E.tensor_tensor(
                    out=ropeB[:], in0=tmpq[:], in1=ca, op=ALU.mult), sem="sDV")
                return p.op("vector", lambda E, o=out_sl: E.tensor_tensor(
                    out=o, in0=ropeA[:], in1=ropeB[:], op=ALU.add), sem="sDV")

            with ExitStack() as kvctx:
                wk_sb = kvctx.enter_context(nc.sbuf_tensor("wk_sb", [128, KC * 512], BF16))
                wv_sb = kvctx.enter_context(nc.sbuf_tensor("wv_sb", [128, KC * 512], BF16))
                p.dma("gpsimd", wk_sb[:], wk_in[0], "sGD")
                p.dma("gpsimd", wv_sb[:], wv_in[0], "sGD")
                p.dwait("tensor", "sGD")
                p.dwait("vector", "sSD")
                kv_free = [0, 0]
                for kf in range(KVH):
                    for t5 in range(5):
                        pi = (kf * 5 + t5) % 2
                        psq = ps[pi]
                        p.wait("tensor", "sDV", kv_free[pi])
                        for kc in range(KC):
                            rhs = xsl(kc, t5)
                            p.op("tensor", lambda E, psq=psq, kf=kf, kc=kc, rhs=rhs,
                                 st=(kc == 0), sp=(kc == KC - 1): E.matmul(
                                     psq[:, :],
                                     wk_sb[:, kc * 512 + kf * 128: kc * 512 + kf * 128 + 128],
                                     rhs, start=st, stop=sp),
                                 sem="sTE" if kc == KC - 1 else None)
                        p.wait("scalar", "sTE", p.cnt["sTE"])
                        p.wait("vector", "sTE", p.cnt["sTE"])
                        kv_free[pi] = qk_evac(
                            psq, kw_col,
                            K_sb[:, kf * TFULL + t5 * 512: kf * TFULL + t5 * 512 + 512],
                            cos_sb[:, t5 * 512:(t5 + 1) * 512],
                            sin_sb[:, t5 * 512:(t5 + 1) * 512])
                psv_free = [p.cnt["sDV"], p.cnt["sDV"]]
                for t20 in range(20):
                    psq = ps[2 + t20 % 2]
                    p.wait("tensor", "sDV", psv_free[t20 % 2])
                    for kc in range(KC):
                        if t20 < 4:
                            lh = xn1h[:, kc * HALO + t20 * 128: kc * HALO + t20 * 128 + 128]
                        else:
                            lh = xn1[:, kc * TCORE + (t20 - 4) * 128:
                                     kc * TCORE + (t20 - 4) * 128 + 128]
                        p.op("tensor", lambda E, psq=psq, kc=kc, lh=lh,
                             st=(kc == 0), sp=(kc == KC - 1): E.matmul(
                                 psq[:, :], lh,
                                 wv_sb[:, kc * 512:(kc + 1) * 512],
                                 start=st, stop=sp),
                             sem="sTE" if kc == KC - 1 else None)
                    p.wait("vector", "sTE", p.cnt["sTE"])
                    psv_free[t20 % 2] = p.op("vector", lambda E, psq=psq, t20=t20:
                                             E.tensor_copy(
                                                 out=V_sb[:, t20 * 512:(t20 + 1) * 512],
                                                 in_=psq[:]), sem="sDV")
                p.barrier()
            if debug:
                p.dma("sync", dbg_k, K_sb[:], "sSD")
                p.dma("sync", dbg_v, V_sb[:], "sSD")
                p.barrier()
            hctx.close()

            wq_sb = [SA(f"wq_sb{i}", [128, D], BF16) for i in range(2)]
            wo_sb = [SA(f"wo_sb{i}", [128, D], BF16) for i in range(2)]
            q4 = SA("q4", [128, 4 * 512], BF16)
            exp_sb = SA("exp_sb", [128, 5 * 512], BF16)
            msk_sb = SA("msk_sb", [128, 20 * 512], BF16)
            ctx_sb = SA("ctx_sb", [128, KC * 512], BF16)
            scf = [SA(f"scf{i}", [128, 512], F32) for i in range(2)]
            xres = [SA(f"xres{i}", [128, 512], F32) for i in range(2)]
            otl = [SA(f"otl{i}", [128, 512], BF16) for i in range(2)]
            q4r = q4[:].rearrange("p (h q) -> p h q", h=4)

            wq_free = [0, 0]
            wo_free = [0, 0]
            xres_read = [0, 0]
            otl_store = [0, 0]
            msk_read = 0
            oproj_done = 0
            exp_read_mm = 0
            for tg in range(4):
                p.wait("sync", "sDV", msk_read)
                for qtl in range(4):
                    qt = tg * 4 + qtl
                    p.dma("sync", msk_sb[:, qtl * 2560:(qtl + 1) * 2560], mask_in[qt], "sSD")
                for kvh in range(KVH):
                    q4_ready = 0
                    for qh in range(4):
                        head = kvh * 4 + qh
                        slot = (tg * 16 + kvh * 4 + qh) % 2
                        p.wait("gpsimd", "sTE", wq_free[slot])
                        p.dma("gpsimd", wq_sb[slot][:], wq_in[head], "sGD")
                        p.dwait("tensor", "sGD")
                        p.wait("tensor", "sDV", q4_ready)
                        for kc in range(KC):
                            p.op("tensor", lambda E, slot=slot, kc=kc, tg=tg,
                                 st=(kc == 0), sp=(kc == KC - 1): E.matmul(
                                     ps[3][:, :],
                                     wq_sb[slot][:, kc * 128:(kc + 1) * 128],
                                     xn1[:, kc * TCORE + tg * 512:
                                         kc * TCORE + tg * 512 + 512],
                                     start=st, stop=sp),
                                 sem="sTE" if kc == KC - 1 else None)
                        wq_free[slot] = p.cnt["sTE"]
                        p.wait("scalar", "sTE", p.cnt["sTE"])
                        p.wait("vector", "sTE", p.cnt["sTE"])
                        q4_ready = qk_evac(
                            ps[3], qw_col, q4[:, qh * 512:(qh + 1) * 512],
                            cos_sb[:, HALO + tg * 512: HALO + tg * 512 + 512],
                            sin_sb[:, HALO + tg * 512: HALO + tg * 512 + 512])
                    for qtl in range(4):
                        qt = tg * 4 + qtl
                        exp_ready = {}
                        for j in range(5):
                            kt = qt + j
                            psS = ps[j % 2]
                            p.wait("tensor", "sDV", p.cnt["sDV"])
                            p.wait("tensor", "sDV", q4_ready)
                            p.dwait("tensor", "sSD")
                            mm_s = p.op("tensor", lambda E, psS=psS, kvh=kvh, kt=kt, qtl=qtl:
                                        E.matmul(psS[:, :],
                                                 K_sb[:, kvh * TFULL + kt * 128:
                                                      kvh * TFULL + kt * 128 + 128],
                                                 q4r[:, :, qtl * 128:(qtl + 1) * 128],
                                                 start=True, stop=True), sem="sTE")
                            p.wait("vector", "sTE", mm_s)
                            sc = scf[j % 2]
                            msk_read = p.op("vector", lambda E, psS=psS, sc=sc, qtl=qtl, j=j:
                                            E.tensor_tensor(
                                                out=sc[:], in0=psS[:],
                                                in1=msk_sb[:, (qtl * 5 + j) * 512:
                                                           (qtl * 5 + j) * 512 + 512],
                                                op=ALU.add), sem="sDV")
                            p.wait("scalar", "sDV", msk_read)
                            p.wait("scalar", "sTE", exp_read_mm)
                            exp_ready[j] = p.op("scalar", lambda E, sc=sc, j=j:
                                                E.activation(
                                                    out=exp_sb[:, j * 512:(j + 1) * 512],
                                                    in_=sc[:], func=AF.Exp), sem="sAC")
                        p.wait("tensor", "sDV", p.cnt["sDV"])
                        for j in range(5):
                            p.wait("tensor", "sAC", exp_ready[j])
                            p.op("tensor", lambda E, j=j: E.matmul(
                                psr0[:, :], ones_col[:], exp_sb[:, j * 512:(j + 1) * 512],
                                start=(j == 0), stop=(j == 4)),
                                sem="sTE" if j == 4 else None)
                        den_done = p.cnt["sTE"]
                        for j in range(5):
                            kt = qt + j
                            for qh in range(4):
                                p.op("tensor", lambda E, j=j, qh=qh, kt=kt, kvh=kvh,
                                     st=(j == 0), sp=(j == 4): E.matmul(
                                         ps[2][:, qh * 128:(qh + 1) * 128],
                                         V_sb[:, kt * 512 + kvh * 128:
                                              kt * 512 + kvh * 128 + 128],
                                         exp_sb[:, j * 512 + qh * 128:
                                                j * 512 + qh * 128 + 128],
                                         start=st, stop=sp),
                                     sem="sTE" if (j == 4 and qh == 3) else None)
                        exp_read_mm = p.cnt["sTE"]
                        p.wait("vector", "sTE", den_done)
                        rc = p.vops(lambda E: E.reciprocal(out=den_r[:, :], in_=psr0[:, :]))
                        p.wait("tensor", "sDV", rc)
                        bmm = p.op("tensor", lambda E: E.matmul(
                            ps[5][:, :], ones_row_f32[:], den_r[:, :],
                            start=True, stop=True), sem="sTE")
                        p.wait("scalar", "sTE", bmm)
                        rb_t = p.op("scalar", lambda E: E.copy(
                            out=recipb[:], in_=ps[5][:, :]), sem="sAC")
                        p.wait("vector", "sAC", rb_t)
                        p.wait("vector", "sTE", exp_read_mm)
                        p.wait("vector", "sTE", oproj_done)
                        for qh in range(4):
                            head = kvh * 4 + qh
                            p.op("vector", lambda E, qh=qh, head=head, qtl=qtl:
                                 E.tensor_tensor(
                                     out=ctx_sb[:, head * 512 + qtl * 128:
                                                head * 512 + qtl * 128 + 128],
                                     in0=ps[2][:, qh * 128:(qh + 1) * 128],
                                     in1=recipb[:, qh * 128:(qh + 1) * 128],
                                     op=ALU.mult), sem="sDV")
                ctx_done = p.cnt["sDV"]
                ps4_free = 0
                for fc in range(KC):
                    slot = fc % 2
                    p.wait("gpsimd", "sTE", wo_free[slot])
                    p.dma("gpsimd", wo_sb[slot][:], wo_in[fc], "sGD")
                    p.wait("sync", "sDV", xres_read[slot])
                    p.dma("sync", xres[slot][:],
                          xfm[fc, :, HALO + tg * 512: HALO + tg * 512 + 512], "sSD")
                    p.dwait("tensor", "sGD")
                    p.wait("tensor", "sDV", ctx_done)
                    p.wait("tensor", "sDV", ps4_free)
                    for hc in range(KC):
                        p.op("tensor", lambda E, slot=slot, hc=hc,
                             st=(hc == 0), sp=(hc == KC - 1): E.matmul(
                                 ps[4][:, :],
                                 wo_sb[slot][:, hc * 128:(hc + 1) * 128],
                                 ctx_sb[:, hc * 512:(hc + 1) * 512],
                                 start=st, stop=sp),
                             sem="sTE" if hc == KC - 1 else None)
                    wo_free[slot] = p.cnt["sTE"]
                    p.wait("vector", "sTE", p.cnt["sTE"])
                    p.dwait("vector", "sSD")
                    p.wait("vector", "sSD", otl_store[slot])
                    fin = p.op("vector", lambda E, slot=slot: E.tensor_tensor(
                        out=otl[slot][:], in0=ps[4][:, :], in1=xres[slot][:],
                        op=ALU.add), sem="sDV")
                    ps4_free = fin
                    xres_read[slot] = fin
                    p.wait("sync", "sDV", fin)
                    p.dma("sync", s_attn[fc, :, tg * 512: tg * 512 + 512],
                          otl[slot][:], "sSD")
                    otl_store[slot] = p.cnt["sSD"]
                oproj_done = p.cnt["sTE"]
            p.barrier()
            if debug:
                for kc in range(KC):
                    p.dma("sync", dbg_sattn[kc], s_attn[kc], "sSD")
                p.barrier()

        with ExitStack() as mctx:
            SM = lambda n_, s, d: mctx.enter_context(nc.sbuf_tensor(n_, s, d))
            xn2 = SM("xn2", [128, KC * TCORE], BF16)
            loads2 = {}
            for kc in range(KC):
                t = p.dma("sync", xn2[:, kc * TCORE:(kc + 1) * TCORE], s_attn[kc], "sSD")
                loads2[kc] = (xn2[:, kc * TCORE:(kc + 1) * TCORE], ("sSD", t))
            with ExitStack() as bctx:
                mu2 = bctx.enter_context(nc.sbuf_tensor("mu2", [1, TCORE], F32))
                r2 = bctx.enter_context(nc.sbuf_tensor("r2", [1, TCORE], F32))
                emit_ln_stats(p, lambda kc, s: (loads2[kc][0][:, s * 512:(s + 1) * 512],
                                                loads2[kc][1]),
                              TCORE, ones_col, psr0, psr1,
                              sq_scr, mu2, r2, tmp_row, eps=1e-5)
                mu2b = bctx.enter_context(nc.sbuf_tensor("mu2b", [128, TCORE], F32))
                r2b = bctx.enter_context(nc.sbuf_tensor("r2b", [128, TCORE], F32))
                t2s = bctx.enter_context(nc.sbuf_tensor("t2s", [128, TCORE], BF16))
                bcast_row(p, mu2, TCORE, ps[5], mu2b, ones_row_f32)
                bc = bcast_row(p, r2, TCORE, ps[5], r2b, ones_row_f32)
                p.wait("vector", "sAC", bc)
                for kc in range(KC):
                    sl = xn2[:, kc * TCORE:(kc + 1) * TCORE]
                    p.op("vector", lambda E, sl=sl: E.tensor_tensor(
                        out=t2s[:], in0=sl, in1=mu2b[:], op=ALU.subtract), sem="sDV")
                    p.op("vector", lambda E, sl=sl: E.tensor_tensor(
                        out=sl, in0=t2s[:], in1=r2b[:], op=ALU.mult), sem="sDV")
                p.barrier()
            if debug:
                for kc in range(KC):
                    p.dma("sync", dbg_xn2[kc], xn2[:, kc * TCORE:(kc + 1) * TCORE], "sSD")

            with ExitStack() as gctx:
                gate_sb = gctx.enter_context(nc.sbuf_tensor("gate_sb", [128, KC * NEXP], BF16))
                cw_fm = gctx.enter_context(nc.sbuf_tensor("cw_fm", [NEXP, TCORE], F32))
                gsc = gctx.enter_context(nc.sbuf_tensor("gsc", [128, NEXP], F32))
                gs4 = gctx.enter_context(nc.sbuf_tensor("gs4", [128, 8], F32))
                gm = gctx.enter_context(nc.sbuf_tensor("gm", [128, 4], F32))
                em = gctx.enter_context(nc.sbuf_tensor("em", [128, NEXP], F32))
                gm8 = gctx.enter_context(nc.sbuf_tensor("gm8", [128, 8], F32))
                gst = gctx.enter_context(nc.sbuf_tensor("gst", [128, 8], F32))
                p.dma("gpsimd", gate_sb[:], gate_in[0], "sGD")
                p.dwait("tensor", "sGD")
                p.op("vector", lambda E: E.memset(gs4[:, 4:8], NEG), sem="sDV")
                tp_free = 0
                for tt in range(KC):
                    p.wait("tensor", "sDV", p.cnt["sDV"])
                    p.wait("tensor", "sAC", p.cnt["sAC"])
                    for kc in range(KC):
                        p.op("tensor", lambda E, tt=tt, kc=kc,
                             st=(kc == 0), sp=(kc == KC - 1): E.matmul(
                                 ps[5][:, 0:NEXP],
                                 xn2[:, kc * TCORE + tt * 128: kc * TCORE + tt * 128 + 128],
                                 gate_sb[:, kc * NEXP:(kc + 1) * NEXP],
                                 start=st, stop=sp),
                             sem="sTE" if kc == KC - 1 else None)
                    p.wait("vector", "sTE", p.cnt["sTE"])
                    p.vops(lambda E: E.reduce_max(
                        out=gst[:, 0:1], in_=ps[5][:, 0:NEXP], axis=AX.X, negate=True))
                    p.wait("scalar", "sDV", p.cnt["sDV"])
                    ex_t = p.op("scalar", lambda E: E.activation(
                        out=gsc[:], in_=ps[5][:, 0:NEXP], func=AF.Exp, bias=gst[:, 0:1],
                        accum_out=gst[:, 1:2]), sem="sAC")
                    p.wait("vector", "sAC", ex_t)
                    p.vops(lambda E: E.reciprocal(out=gst[:, 2:3], in_=gst[:, 1:2]))
                    p.vops(lambda E: E.tensor_scalar(
                        out=gsc[:], in0=gsc[:], scalar1=gst[:, 2:3], scalar2=None,
                        op0=ALU.mult))
                    p.vops(lambda E: E.tensor_reduce(
                        out=gs4[:, 0:4], in_=gsc[:].rearrange("p (g j) -> p g j", j=4),
                        axis=AX.X, op=ALU.max))
                    p.vops(lambda E: E.max(out=gm8[:], in_=gs4[:]))
                    p.vops(lambda E: E.tensor_scalar(
                        out=gm[:], in0=gs4[:, 0:4], scalar1=gm8[:, 1:2], scalar2=None,
                        op0=ALU.is_ge))
                    for j in range(4):
                        p.vops(lambda E, j=j: E.tensor_copy(
                            out=em[:].rearrange("p (g j) -> p g j", j=4)[:, :, j],
                            in_=gm[:]))
                    p.vops(lambda E: E.tensor_tensor(
                        out=em[:], in0=em[:], in1=gsc[:], op=ALU.mult))
                    p.vops(lambda E: E.max(out=gm8[:], in_=em[:]))
                    p.vops(lambda E: E.scalar_tensor_tensor(
                        out=em[:], in0=em[:], scalar=gm8[:, 3:4], in1=em[:],
                        op0=ALU.is_ge, op1=ALU.mult))
                    p.vops(lambda E: E.tensor_reduce(
                        out=gst[:, 3:4], in_=em[:], axis=AX.X, op=ALU.add))
                    p.vops(lambda E: E.tensor_scalar(
                        out=gst[:, 3:4], in0=gst[:, 3:4], scalar1=1e-20, scalar2=None,
                        op0=ALU.add))
                    p.vops(lambda E: E.reciprocal(out=gst[:, 4:5], in_=gst[:, 3:4]))
                    cw_t = p.vops(lambda E: E.tensor_scalar(
                        out=em[:], in0=em[:], scalar1=gst[:, 4:5], scalar2=None,
                        op0=ALU.mult))
                    p.wait("tensor", "sDV", cw_t)
                    p.wait("tensor", "sDV", tp_free)
                    mt = p.op("tensor", lambda E: E.transpose(
                        ps[4][0:NEXP, 0:128], em[:], ident_f32[:]), sem="sTE")
                    p.wait("vector", "sTE", mt)
                    tp_free = p.op("vector", lambda E, tt=tt: E.tensor_copy(
                        out=cw_fm[:, tt * 128:(tt + 1) * 128], in_=ps[4][0:NEXP, 0:128]),
                        sem="sDV")
                p.wait("sync", "sDV", p.cnt["sDV"])
                p.dma("sync", cw_dram, cw_fm[:], "sSD")
                if debug:
                    p.dma("sync", dbg_cw, cw_fm[:], "sSD")
                p.barrier()

            h_pool = SM("h_pool", [128, 33 * TMACRO], BF16)
            acc = SM("acc", [128, KC * TMACRO], BF16)
            cwb = [SM(f"cwb{i}", [128, TMACRO], BF16) for i in range(3)]
            wup = [SM(f"wup{i}", [128, D], BF16) for i in range(4)]
            wdn = [SM(f"wdn{i}", [128, 3 * IEXP], BF16) for i in range(2)]
            sres = [sq_scr[0]] * 2
            t_sil = [sq_scr[0]] * 2
            t_su = [sq_scr[1]] * 2

            wup_free = [0, 0, 0, 0]
            wdn_free = [0, 0]
            h_read = 0
            acc_read = 0
            for mc in range(2):
                t0m = mc * TMACRO
                for pi, group in enumerate(MOE_GROUPS):
                    shared = (pi == 5)
                    cwb_ready = 0
                    for gi, e in enumerate(group):
                        for sh_ in range(2):
                            p.wait("sync", "sTE", p.cnt["sTE"])
                            p.dma("sync", tmp_row[:],
                                  cw_dram[e:e + 1, t0m + sh_ * 512: t0m + sh_ * 512 + 512],
                                  "sSD")
                            cwb_ready = bcast_row(
                                p, tmp_row, 512, ps[5],
                                cwb[gi][:, sh_ * 512: sh_ * 512 + 512],
                                ones_row_f32,
                                wait_sems=[("sSD", None),
                                           ("sDV", p.cnt["sDV"]),
                                           ("sAC", p.cnt["sAC"])])
                    up_sil_done = 0
                    npair = 11 if not shared else 22
                    for i in range(npair):
                        for gi, e in enumerate(group):
                            sl_g = (2 * (i * len(group) + gi)) % 4
                            sl_u = sl_g + 1
                            p.wait("gpsimd", "sTE", wup_free[sl_g])
                            if shared:
                                p.dma("gpsimd", wup[sl_g][:], sup_in[i], "sGD")
                                p.dma("gpsimd", wup[sl_u][:], sup_in[22 + i], "sGD")
                            else:
                                p.dma("gpsimd", wup[sl_g][:], eup_in[e, i], "sGD")
                                p.dma("gpsimd", wup[sl_u][:], eup_in[e, 11 + i], "sGD")
                            p.dwait("tensor", "sGD")
                            for th in range(2):
                                psA = ps[th]
                                psB = ps[2 + th]
                                p.wait("tensor", "sDV", p.cnt["sDV"])
                                p.wait("tensor", "sAC", p.cnt["sAC"])
                                for kc in range(KC):
                                    p.op("tensor", lambda E, sl=sl_g, kc=kc, th=th, psA=psA, t0m=t0m,
                                         st=(kc == 0), sp=(kc == KC - 1): E.matmul(
                                             psA[:, :],
                                             wup[sl][:, kc * 128:(kc + 1) * 128],
                                             xn2[:, kc * TCORE + t0m + th * 512:
                                                 kc * TCORE + t0m + th * 512 + 512],
                                             start=st, stop=sp),
                                         sem=None)
                                for kc in range(KC):
                                    p.op("tensor", lambda E, sl=sl_u, kc=kc, th=th, psB=psB, t0m=t0m,
                                         st=(kc == 0), sp=(kc == KC - 1): E.matmul(
                                             psB[:, :],
                                             wup[sl][:, kc * 128:(kc + 1) * 128],
                                             xn2[:, kc * TCORE + t0m + th * 512:
                                                 kc * TCORE + t0m + th * 512 + 512],
                                             start=st, stop=sp),
                                         sem="sTE" if kc == KC - 1 else None)
                                mm_up = p.cnt["sTE"]
                                p.wait("scalar", "sTE", mm_up)
                                p.wait("scalar", "sDV", up_sil_done)
                                p.wait("scalar", "sTE", h_read)
                                sil = p.op("scalar", lambda E, th=th, psA=psA: E.activation(
                                    out=t_sil[th][:], in_=psA[:], func=AF.Silu), sem="sAC")
                                p.wait("vector", "sAC", sil)
                                p.op("vector", lambda E, th=th, psB=psB: E.tensor_tensor(
                                    out=t_su[th][:], in0=t_sil[th][:], in1=psB[:],
                                    op=ALU.mult), sem="sDV")
                                if shared:
                                    hoff = (11 + i) * TMACRO + th * 512
                                    up_sil_done = p.op("vector", lambda E, th=th, hoff=hoff:
                                                       E.tensor_copy(
                                                           out=h_pool[:, hoff:hoff + 512],
                                                           in_=t_su[th][:]), sem="sDV")
                                else:
                                    hoff = (gi * 11 + i) * TMACRO + th * 512
                                    p.wait("vector", "sAC", cwb_ready)
                                    up_sil_done = p.op("vector", lambda E, th=th, gi=gi, hoff=hoff:
                                                       E.tensor_tensor(
                                                           out=h_pool[:, hoff:hoff + 512],
                                                           in0=t_su[th][:], in1=cwb[gi][:, th * 512:
                                                                                        th * 512 + 512],
                                                           op=ALU.mult), sem="sDV")
                            wup_free[sl_g] = p.cnt["sTE"]
                            wup_free[sl_u] = p.cnt["sTE"]
                        if shared:
                            pass
                    if shared:
                        e15 = 15
                        for i in range(11):
                            sl_g = (2 * i) % 4
                            sl_u = sl_g + 1
                            p.wait("gpsimd", "sTE", wup_free[sl_g])
                            p.dma("gpsimd", wup[sl_g][:], eup_in[e15, i], "sGD")
                            p.dma("gpsimd", wup[sl_u][:], eup_in[e15, 11 + i], "sGD")
                            p.dwait("tensor", "sGD")
                            for th in range(2):
                                psA = ps[th]
                                psB = ps[2 + th]
                                p.wait("tensor", "sDV", p.cnt["sDV"])
                                for kc in range(KC):
                                    p.op("tensor", lambda E, sl=sl_g, kc=kc, th=th, psA=psA, t0m=t0m,
                                         st=(kc == 0), sp=(kc == KC - 1): E.matmul(
                                             psA[:, :], wup[sl][:, kc * 128:(kc + 1) * 128],
                                             xn2[:, kc * TCORE + t0m + th * 512:
                                                 kc * TCORE + t0m + th * 512 + 512],
                                             start=st, stop=sp), sem=None)
                                for kc in range(KC):
                                    p.op("tensor", lambda E, sl=sl_u, kc=kc, th=th, psB=psB, t0m=t0m,
                                         st=(kc == 0), sp=(kc == KC - 1): E.matmul(
                                             psB[:, :], wup[sl][:, kc * 128:(kc + 1) * 128],
                                             xn2[:, kc * TCORE + t0m + th * 512:
                                                 kc * TCORE + t0m + th * 512 + 512],
                                             start=st, stop=sp),
                                         sem="sTE" if kc == KC - 1 else None)
                                p.wait("scalar", "sTE", p.cnt["sTE"])
                                p.wait("scalar", "sDV", up_sil_done)
                                p.wait("scalar", "sTE", h_read)
                                sil = p.op("scalar", lambda E, th=th, psA=psA: E.activation(
                                    out=t_sil[th][:], in_=psA[:], func=AF.Silu), sem="sAC")
                                p.wait("vector", "sAC", sil)
                                p.op("vector", lambda E, th=th, psB=psB: E.tensor_tensor(
                                    out=t_su[th][:], in0=t_sil[th][:], in1=psB[:],
                                    op=ALU.mult), sem="sDV")
                                hoff = i * TMACRO + th * 512
                                p.wait("vector", "sAC", cwb_ready)
                                up_sil_done = p.op("vector", lambda E, th=th, hoff=hoff:
                                                   E.tensor_tensor(
                                                       out=h_pool[:, hoff:hoff + 512],
                                                       in0=t_su[th][:], in1=cwb[0][:, th * 512:
                                                                                   th * 512 + 512],
                                                       op=ALU.mult), sem="sDV")
                            wup_free[sl_g] = p.cnt["sTE"]
                            wup_free[sl_u] = p.cnt["sTE"]
                    up_done = p.cnt["sDV"]
                    for hc in range(KC):
                        dn_specs = []
                        for gi, e in enumerate(group):
                            dn_specs.append(("e", e, gi))
                        if shared:
                            dn_specs.append(("s", None, None))
                        slot = hc % 2
                        p.wait("gpsimd", "sTE", wdn_free[slot])
                        nblk = 0
                        for kind, e, gi in dn_specs:
                            if kind == "e":
                                p.dma("gpsimd", wdn[slot][:, nblk * IEXP: nblk * IEXP + IEXP],
                                      edn_in[e, hc], "sGD")
                                nblk += 1
                            else:
                                p.dma("gpsimd", wdn[slot][:, IEXP:IEXP + SI], sdn_in[hc], "sGD")
                        p.dwait("tensor", "sGD")
                        for th in range(2):
                            psD = ps[4 + th]
                            p.wait("tensor", "sDV", p.cnt["sDV"])
                            p.wait("tensor", "sAC", p.cnt["sAC"])
                            p.wait("tensor", "sDV", up_done)
                            total_mm = sum(11 if k == "e" else 22 for k, _, _ in dn_specs)
                            flat = 0
                            nblk = 0
                            for kind, e, gi in dn_specs:
                                if kind == "e":
                                    base = (0 if shared else gi * 11)
                                    nic = 11
                                    woff = nblk * IEXP
                                    nblk += 1
                                else:
                                    base = 11
                                    nic = 22
                                    woff = IEXP
                                for ic in range(nic):
                                    st_f = (flat == 0)
                                    sp_f = (flat == total_mm - 1)
                                    p.op("tensor", lambda E, woff=woff, ic=ic, th=th, psD=psD,
                                         slot=slot, base=base,
                                         st=st_f, sp=sp_f: E.matmul(
                                             psD[:, :],
                                             wdn[slot][:, woff + ic * 128: woff + ic * 128 + 128],
                                             h_pool[:, (base + ic) * TMACRO + th * 512:
                                                    (base + ic) * TMACRO + th * 512 + 512],
                                             start=st, stop=sp),
                                         sem="sTE" if sp_f else None)
                                    flat += 1
                            mm_dn = p.cnt["sTE"]
                            aoff = hc * TMACRO + th * 512
                            if pi == 0:
                                p.wait("scalar", "sTE", mm_dn)
                                p.op("scalar", lambda E, aoff=aoff, psD=psD: E.copy(
                                    out=acc[:, aoff:aoff + 512], in_=psD[:]), sem="sAC")
                            elif pi < 5:
                                p.wait("vector", "sTE", mm_dn)
                                p.wait("vector", "sAC", p.cnt["sAC"])
                                p.op("vector", lambda E, aoff=aoff, psD=psD: E.tensor_tensor(
                                    out=acc[:, aoff:aoff + 512], in0=acc[:, aoff:aoff + 512],
                                    in1=psD[:], op=ALU.add), sem="sDV")
                            else:
                                oslot = (hc * 2 + th) % 2
                                p.wait("sync", "sDV", p.cnt["sDV"])
                                p.wait("sync", "sAC", p.cnt["sAC"])
                                p.dma("sync", sres[oslot][:],
                                      s_attn[hc, :, t0m + th * 512: t0m + th * 512 + 512], "sSD")
                                p.wait("vector", "sTE", mm_dn)
                                p.wait("vector", "sAC", p.cnt["sAC"])
                                p.dwait("vector", "sSD")
                                t1 = p.op("vector", lambda E, psD=psD, oslot=oslot:
                                          E.tensor_tensor(
                                              out=sres[oslot][:], in0=sres[oslot][:],
                                              in1=psD[:], op=ALU.add), sem="sDV")
                                fin = p.op("vector", lambda E, aoff=aoff, oslot=oslot:
                                           E.tensor_tensor(
                                               out=sres[oslot][:], in0=sres[oslot][:],
                                               in1=acc[:, aoff:aoff + 512],
                                               op=ALU.add), sem="sDV")
                                acc_read = fin
                                p.wait("sync", "sDV", fin)
                                p.dma("sync", stream_out[hc, :, t0m + th * 512:
                                                          t0m + th * 512 + 512],
                                      sres[oslot][:], "sSD")
                        wdn_free[slot] = p.cnt["sTE"]
                    h_read = p.cnt["sTE"]
                p.barrier()
                if debug and mc == 0:
                    for hc in range(KC):
                        p.dma("sync", dbg_sm0[hc], stream_out[hc, :, 0:TMACRO], "sSD")
                    p.barrier()

        with ExitStack() as pctx:
            SP = lambda n_, s, d: pctx.enter_context(nc.sbuf_tensor(n_, s, d))
            xs3 = SP("xs3", [128, KC * TCORE], BF16)
            mu3 = SP("mu3", [1, TCORE], F32)
            r3 = SP("r3", [1, TCORE], F32)
            a_sb = SP("a_sb", [128, KC * NS], F32)
            state = SP("state", [128, KC * NS], F32)
            bdt_sb = SP("bdt_sb", [128, KC], F32)
            p.op("vector", lambda E: E.memset(state[:], 0.0), sem="sDV")
            p.dma("sync", bdt_sb[:], bdt_in, "sSD")
            for kc in range(KC):
                p.dma("sync", a_sb[:, kc * NS:(kc + 1) * NS], alog_in[kc], "sSD")
            p.dwait("scalar", "sSD")
            ac_t = p.op("scalar", lambda E: E.activation(
                out=a_sb[:], in_=a_sb[:], func=AF.Exp), sem="sAC")
            p.wait("vector", "sAC", ac_t)
            p.op("vector", lambda E: E.tensor_scalar(
                out=a_sb[:], in0=a_sb[:], scalar1=-1.0, scalar2=None,
                op0=ALU.mult), sem="sDV")
            loads3 = {}
            for kc in range(KC):
                t = p.dma("sync", xs3[:, kc * TCORE:(kc + 1) * TCORE], stream_out[kc], "sSD")
                loads3[kc] = (xs3[:, kc * TCORE:(kc + 1) * TCORE], ("sSD", t))
            emit_ln_stats(p, lambda kc, s: (loads3[kc][0][:, s * 512:(s + 1) * 512],
                                            loads3[kc][1]),
                          TCORE, ones_col, psr0, psr1,
                          sq_scr, mu3, r3, tmp_row, eps=1e-5)
            p.barrier()

            xn3 = SP("xn3", [128, KC * TCH], BF16)
            dt_sb = SP("dt_sb", [128, KC * TCH], F32)
            w_sb = SP("w_sb", [128, KC * TCH], BF16)
            mu3b = SP("mu3b", [128, TCH], F32)
            r3b = SP("r3b", [128, TCH], F32)
            a_t = [SP(f"a_t{i}", [128, TCH], F32) for i in range(2)]
            g_t = [SP(f"g_t{i}", [128, TCH], F32) for i in range(2)]
            x_t = [SP(f"x_t{i}", [128, TCH], F32) for i in range(2)]
            bb = SP("bb", [128, TCH], F32)
            b_row = SP("b_row", [1, TCH], F32)
            dt1_sb = SP("dt1_sb", [RDT, TCH], BF16)
            bc_tmp = SP("bc_tmp", [NS, TCH], F32)
            wdtin_sb = SP("wdtin_sb", [128, KC * RDT], BF16)
            wdtout_sb = SP("wdtout_sb", [RDT, D], BF16)
            wb_sb = SP("wb_sb", [128, KC * NS], BF16)
            wc_sb = SP("wc_sb", [128, KC * NS], BF16)
            dtp = SP("dtp", [128, 512], F32)
            spa = SP("spa", [128, 512], BF16)
            spb = SP("spb", [128, 512], BF16)
            p.dma("gpsimd", wdtin_sb[:], wdtin_in[0], "sGD")
            p.dma("gpsimd", wdtout_sb[:], wdtout_in, "sGD")
            p.dma("gpsimd", wb_sb[:], wb_in[0], "sGD")
            p.dma("gpsimd", wc_sb[:], wc_in[0], "sGD")

            for tc in range(NTCH):
                t0 = tc * TCH
                bcast_row(p, mu3[:, t0:t0 + TCH], TCH, ps[5], mu3b, ones_row_f32)
                bc = bcast_row(p, r3[:, t0:t0 + TCH], TCH, ps[5], r3b, ones_row_f32)
                p.wait("vector", "sAC", bc)
                p.wait("vector", "sTE", p.cnt["sTE"])
                for kc in range(KC):
                    sl = xn3[:, kc * TCH:(kc + 1) * TCH]
                    p.op("vector", lambda E, kc=kc, t0=t0: E.tensor_tensor(
                        out=a_t[0][:], in0=xs3[:, kc * TCORE + t0: kc * TCORE + t0 + TCH],
                        in1=mu3b[:], op=ALU.subtract), sem="sDV")
                    p.op("vector", lambda E, sl=sl: E.tensor_tensor(
                        out=sl, in0=a_t[0][:], in1=r3b[:], op=ALU.mult), sem="sDV")
                xn3_done = p.cnt["sDV"]
                p.dwait("tensor", "sGD")
                p.wait("tensor", "sDV", xn3_done)
                for th in range(TCH // 512):
                    p.wait("tensor", "sDV", p.cnt["sDV"])
                    for kc in range(KC):
                        p.op("tensor", lambda E, kc=kc, th=th,
                             st=(kc == 0), sp=(kc == KC - 1): E.matmul(
                                 ps[0][0:RDT, :],
                                 wdtin_sb[:, kc * RDT:(kc + 1) * RDT],
                                 xn3[:, kc * TCH + th * 512: kc * TCH + th * 512 + 512],
                                 start=st, stop=sp),
                             sem="sTE" if kc == KC - 1 else None)
                    p.wait("vector", "sTE", p.cnt["sTE"])
                    p.op("vector", lambda E, th=th: E.tensor_copy(
                        out=dt1_sb[:, th * 512: th * 512 + 512], in_=ps[0][0:RDT, :]),
                        sem="sDV")
                    for kc in range(KC):
                        p.op("tensor", lambda E, kc=kc, th=th,
                             st=(kc == 0), sp=(kc == KC - 1): E.matmul(
                                 ps[1][0:NS, :],
                                 wb_sb[:, kc * NS:(kc + 1) * NS],
                                 xn3[:, kc * TCH + th * 512: kc * TCH + th * 512 + 512],
                                 start=st, stop=sp),
                             sem="sTE" if kc == KC - 1 else None)
                    p.wait("vector", "sTE", p.cnt["sTE"])
                    bv = p.op("vector", lambda E: E.tensor_copy(
                        out=bc_tmp[:], in_=ps[1][0:NS, :]), sem="sDV")
                    p.wait("sync", "sDV", bv)
                    p.dma("sync", b_out[:, t0 + th * 512: t0 + th * 512 + 512],
                          bc_tmp[:], "sSD")
                    p.wait("tensor", "sDV", bv)
                    for kc in range(KC):
                        p.op("tensor", lambda E, kc=kc, th=th,
                             st=(kc == 0), sp=(kc == KC - 1): E.matmul(
                                 ps[1][0:NS, :],
                                 wc_sb[:, kc * NS:(kc + 1) * NS],
                                 xn3[:, kc * TCH + th * 512: kc * TCH + th * 512 + 512],
                                 start=st, stop=sp),
                             sem="sTE" if kc == KC - 1 else None)
                    p.wait("vector", "sTE", p.cnt["sTE"])
                    p.dwait("vector", "sSD")
                    cv = p.op("vector", lambda E: E.tensor_copy(
                        out=bc_tmp[:], in_=ps[1][0:NS, :]), sem="sDV")
                    p.wait("sync", "sDV", cv)
                    p.dma("sync", c_out[:, t0 + th * 512: t0 + th * 512 + 512],
                          bc_tmp[:], "sSD")
                    dt1_ready = p.cnt["sDV"]
                    for hc in range(KC):
                        psD = ps[2 + hc % 2]
                        p.wait("tensor", "sDV", dt1_ready)
                        p.wait("tensor", "sDV", p.cnt["sDV"])
                        mmd = p.op("tensor", lambda E, hc=hc, th=th, psD=psD: E.matmul(
                            psD[:, :], wdtout_sb[:, hc * 128:(hc + 1) * 128],
                            dt1_sb[:, th * 512: th * 512 + 512],
                            start=True, stop=True), sem="sTE")
                        p.wait("vector", "sTE", mmd)
                        p.op("vector", lambda E, hc=hc, psD=psD: E.tensor_scalar(
                            out=dtp[:], in0=psD[:], scalar1=bdt_sb[:, hc:hc + 1],
                            scalar2=None, op0=ALU.add), sem="sDV")
                        dv_t = p.cnt["sDV"]
                        p.wait("scalar", "sDV", dv_t)
                        p.aops(lambda E: E.activation(out=spa[:], in_=dtp[:], func=AF.Abs))
                        p.aops(lambda E: E.activation(out=spb[:], in_=spa[:], func=AF.Exp,
                                                      scale=-1.0))
                        ln_t = p.aops(lambda E: E.activation(out=spa[:], in_=spb[:],
                                                             func=AF.Ln, bias=1.0))
                        p.wait("vector", "sAC", ln_t)
                        p.vops(lambda E: E.tensor_scalar(
                            out=dtp[:], in0=dtp[:], scalar1=0.0, scalar2=None,
                            op0=ALU.max))
                        dtv = p.op("vector", lambda E, hc=hc, th=th: E.tensor_tensor(
                            out=dt_sb[:, hc * TCH + th * 512: hc * TCH + th * 512 + 512],
                            in0=dtp[:], in1=spa[:], op=ALU.add), sem="sDV")
                        p.wait("sync", "sDV", dtv)
                        p.dma("sync", dt_out[hc, :, t0 + th * 512: t0 + th * 512 + 512],
                              dt_sb[:, hc * TCH + th * 512: hc * TCH + th * 512 + 512],
                              "sSD")
                wdone0 = p.cnt["sDV"]
                for kc in range(KC):
                    p.op("vector", lambda E, kc=kc: E.tensor_tensor(
                        out=w_sb[:, kc * TCH:(kc + 1) * TCH],
                        in0=dt_sb[:, kc * TCH:(kc + 1) * TCH],
                        in1=xn3[:, kc * TCH:(kc + 1) * TCH], op=ALU.mult), sem="sDV")
                prep_done = p.cnt["sDV"]
                scan_done = {}
                pool_done = {}
                for n in range(NS):
                    p.wait("sync", "sTE", p.cnt["sTE"])
                    p.dma("sync", b_row[:], b_out[n:n + 1, t0:t0 + TCH], "sSD")
                    bcn = bcast_row(p, b_row, TCH, ps[5], bb, ones_row_f32,
                                    wait_sems=[("sSD", None), ("sDV", p.cnt["sDV"])])
                    for g in range(KC):
                        ii = n * KC + g
                        ab = a_t[ii % 2]
                        gb = g_t[ii % 2]
                        xb = x_t[ii % 2]
                        col = g * NS + n
                        if ii >= 2:
                            p.wait("scalar", "sDV", scan_done[ii - 2])
                        else:
                            p.wait("scalar", "sDV", prep_done)
                        at_t = p.op("scalar", lambda E, ab=ab, g=g, col=col:
                                    E.activation(out=ab[:],
                                                 in_=dt_sb[:, g * TCH:(g + 1) * TCH],
                                                 func=AF.Exp,
                                                 scale=a_sb[:, col:col + 1]), sem="sAC")
                        p.wait("vector", "sAC", bcn)
                        p.op("vector", lambda E, gb=gb, g=g: E.tensor_tensor(
                            out=gb[:], in0=w_sb[:, g * TCH:(g + 1) * TCH], in1=bb[:],
                            op=ALU.mult), sem="sDV")
                        p.wait("vector", "sAC", at_t)
                        if ii >= 2:
                            p.wait("vector", "sPL", pool_done[ii - 2])
                        sc_t = p.op("vector", lambda E, xb=xb, ab=ab, gb=gb, col=col:
                                    E.tensor_tensor_scan(
                                        out=xb[:], data0=ab[:], data1=gb[:],
                                        initial=state[:, col:col + 1],
                                        op0=ALU.mult, op1=ALU.add), sem="sDV")
                        scan_done[ii] = sc_t
                        p.wait("gpsimd", "sDV", sc_t)
                        pool_done[ii] = p.op("gpsimd", lambda E, xb=xb, col=col:
                                             E.tensor_copy(out=state[:, col:col + 1],
                                                           in_=xb[:, TCH - 1:TCH]), sem="sPL")
                p.barrier()
            p.wait("sync", "sPL", p.cnt["sPL"])
            p.dma("sync", s_out, state[:], "sSD")
            p.barrier()

        block = ctx.enter_context(nc.Block())
        p.run(block)
    return nc



def _l2_shared_inputs(inputs):
    rw = np.ascontiguousarray(inputs["L1_ffn_router_weight"].astype(np.float32).T)
    rw_b = blockify(rw, E2)
    rb = np.ascontiguousarray(inputs["L1_ffn_router_bias"].astype(np.float32).reshape(1, E2))
    eb = np.ascontiguousarray(inputs["L1_ffn_expert_bias"].astype(np.float32))
    alog = np.ascontiguousarray(inputs["L1_attn_A_log"].astype(np.float32).reshape(KC, 128, NS))
    dvec = np.ascontiguousarray(inputs["L1_attn_D"].astype(np.float32).reshape(KC, 128).T)
    ident = np.eye(128, dtype=np.float32)
    return dict(rw_in=rw_b, rb_in=rb, eb_in=eb, alog_in=alog, dvec_in=dvec,
                ident_in=ident)


def _l1_shared_inputs(inputs):
    f32 = np.float32
    wq = blockify(np.ascontiguousarray(inputs["L0_attn_Wq"].astype(f32).T), 128)
    wk = blockify(np.ascontiguousarray(inputs["L0_attn_Wk"].astype(f32).T), 512)
    wv = blockify(np.ascontiguousarray(inputs["L0_attn_Wv"].astype(f32).T), 512)
    wo = blockify(np.ascontiguousarray(inputs["L0_attn_Wo"].astype(f32).T), 128)
    gate = blockify(np.ascontiguousarray(inputs["L0_ffn_gate_w"].astype(f32).T), NEXP)
    eup = np.stack([blockify(np.ascontiguousarray(inputs["L0_ffn_expert_up_w"][e].astype(f32).T), 128)
                    for e in range(NEXP)])
    edn = np.stack([blockify(np.ascontiguousarray(inputs["L0_ffn_expert_down_w"][e].astype(f32).T), 128)
                    for e in range(NEXP)])
    sup = blockify(np.ascontiguousarray(inputs["L0_ffn_shared_up_w"].astype(f32).T), 128)
    sdn = blockify(np.ascontiguousarray(inputs["L0_ffn_shared_down_w"].astype(f32).T), 128)
    wdtin = blockify(np.ascontiguousarray(inputs["L1_attn_W_dt_in"].astype(f32).T), RDT)
    wdtout = np.ascontiguousarray(inputs["L1_attn_W_dt_out"].astype(f32).T)
    bdt = np.ascontiguousarray(inputs["L1_attn_b_dt_out"].astype(f32).reshape(KC, 128).T)
    wb = blockify(np.ascontiguousarray(inputs["L1_attn_W_B"].astype(f32).T), NS)
    wc = blockify(np.ascontiguousarray(inputs["L1_attn_W_C"].astype(f32).T), NS)
    alog = np.ascontiguousarray(inputs["L1_attn_A_log"].astype(f32).reshape(KC, 128, NS))
    qn = np.ascontiguousarray(inputs["L0_attn_qn_w"].astype(f32).reshape(128, 1))
    kn = np.ascontiguousarray(inputs["L0_attn_kn_w"].astype(f32).reshape(128, 1))
    ident = np.eye(128, dtype=f32)
    rotm = np.zeros((128, 128), dtype=f32)
    for i in range(64):
        rotm[i + 64, i] = -1.0
        rotm[i, i + 64] = 1.0
    return dict(wq_in=wq, wk_in=wk, wv_in=wv, wo_in=wo, qn_in=qn, kn_in=kn,
                rotm_in=rotm,
                gate_in=gate, eup_in=eup, edn_in=edn, sup_in=sup, sdn_in=sdn,
                wdtin_in=wdtin, wdtout_in=wdtout, bdt_in=bdt, wb_in=wb,
                wc_in=wc, alog_in=alog, ident_in=ident)


def _rope_tables(half):
    t0 = half * TCORE
    pos = np.clip(np.arange(t0 - HALO, t0 + TCORE), 0, None).astype(np.float32)
    inv = 1.0 / (10000.0 ** (np.arange(0, DH, 2, dtype=np.float32) / DH))
    freqs = pos[:, None] * inv[None, :]
    emb = np.concatenate([freqs, freqs], axis=-1)
    cos = np.ascontiguousarray(np.cos(emb).T).astype(ml_dtypes.bfloat16)
    sin = np.ascontiguousarray(np.sin(emb).T).astype(ml_dtypes.bfloat16)
    return cos, sin


def _masks(half):
    t0 = half * TCORE
    out = np.full((16, 128, 5 * 512), NEG, dtype=np.float32)
    kp = np.arange(128)
    q = np.arange(128)
    for qt in range(16):
        for j in range(5):
            d = 512 - 128 * j + q[None, :] - kp[:, None]
            kg = t0 - HALO + 128 * (qt + j) + kp[:, None] + 0 * q[None, :]
            ok = (d >= 0) & (d <= WINDOW) & (kg >= 0)
            tile = np.where(ok, 0.0, NEG).astype(np.float32)
            for h in range(4):
                out[qt, :, j * 512 + h * 128: j * 512 + h * 128 + 128] = tile
    return out.astype(ml_dtypes.bfloat16)


def kernel(**inputs):
    x = np.asarray(inputs["L0_attn_x"], dtype=np.float32)
    nb, T, _ = x.shape
    xT = np.ascontiguousarray(x.transpose(0, 2, 1)).reshape(nb, KC, 128, T)

    sh1 = _l1_shared_inputs(inputs)
    masks = [_masks(0), _masks(1)]
    ropes = [_rope_tables(0), _rope_tables(1)]
    in_maps1 = []
    for c in range(8):
        b, half = c // 2, c % 2
        t0 = half * TCORE
        if half == 0:
            halo = np.zeros((KC, 128, HALO), np.float32)
            xfm = np.concatenate([halo, xT[b][:, :, 0:TCORE]], axis=2)
        else:
            xfm = xT[b][:, :, t0 - HALO: t0 + TCORE]
        im = dict(xfm=np.ascontiguousarray(xfm),
                  cos_in=ropes[half][0], sin_in=ropes[half][1],
                  mask_in=masks[half])
        im.update(sh1)
        in_maps1.append(im)
    nc1 = build_launch1()
    res1 = run_bass_kernel_spmd(nc1, in_maps1, core_ids=list(range(8)))

    sh2 = _l2_shared_inputs(inputs)
    zero_state = np.zeros((128, KC * NS), np.float32)
    in_maps2 = []
    for c in range(8):
        r1 = res1.results[c]
        s_in = res1.results[c - 1]["s_out"] if c % 2 == 1 else zero_state
        im = dict(stream_in=r1["stream_out"], dt_in=r1["dt_out"],
                  b_in=r1["b_out"], c_in=r1["c_out"], s_in=s_in)
        im.update(sh2)
        in_maps2.append(im)
    nc2 = build_launch2()
    res2 = run_bass_kernel_spmd(nc2, in_maps2, core_ids=list(range(8)))

    out = np.empty((nb, T, D), np.float32)
    for c in range(8):
        b, half = c // 2, c % 2
        t0 = half * TCORE
        out[b, t0:t0 + TCORE, :] = res2.results[c]["out"]
    return out
